# version 05 (final)
# baseline (speedup 1.0000x reference)
import numpy as np
import ml_dtypes
import bass_rust
import concourse.bass as bass
import concourse.mybir as mybir
from concourse.bass_utils import run_bass_kernel_spmd
from concourse.aot_env import allow_ici_all_to_all

F32 = mybir.dt.float32
BF16 = mybir.dt.bfloat16
F8 = mybir.dt.float8e4
AF = mybir.ActivationFunctionType
ALU = mybir.AluOpType
AX = mybir.AxisListType
BF = ml_dtypes.bfloat16

D = 1024
S = 8192
B = 2
NCORE = 8
RPC = 2048
NT = 16
H = 16
HD = 64
FF = 4096
DL = 256
EPS = 1e-5
MAGIC = 12582912.0
EXP_SHIFT = -2.0
NQC = 16
PHASES = 6
DBGTAPS = 0
SUBPA = 99
NOCC = 0

SH1 = 3 * 128 * RPC
SH2 = RPC * 128


def mkap(ap, pattern, extra_offset=0):
    c = ap.copy()
    c.ap = bass_rust.VecI64Pair(pattern)
    c.offset = c.offset + extra_offset
    return c


class Plan:
    ENGS = ("tensor", "vector", "scalar", "gpsimd", "sync")

    def __init__(self):
        self.ops = {e: [] for e in self.ENGS}

    def add(self, eng, fn):
        self.ops[eng].append(fn)

    def run(self, block):
        for eng in self.ENGS:
            thunks = self.ops[eng]
            if not thunks:
                continue
            def body(e, thunks=thunks):
                for t in thunks:
                    t(e)
            getattr(block, eng)(body)


def build(phases=None):
    if phases is None:
        phases = PHASES
    nc = bass.Bass("TRN2", target_bir_lowering=False, debug=False)

    def din(name, shape, dt=F32):
        return nc.dram_tensor(name, shape, dt, kind="ExternalInput").ap()

    x_in = din("x_rows", [RPC, D])
    wq_d = din("wq", [D, D]); wk_d = din("wk", [D, D]); wv_d = din("wv", [D, D])
    wo_d = din("wo", [D, D])
    w1t_d = din("w1t", [D, FF]); w2t_d = din("w2t", [FF, D])
    b1c_d = din("b1col", [128, 32])
    brows_d = din("brows", [1, 6 * D], BF16)
    wqb_d = din("wqb", [D, D]); wdn_d = din("wdn", [D, DL])
    wup_d = din("wup", [DL, 2 * D]); wob_d = din("wob", [D, D])
    rel2_d = din("rel2", [32, 2])
    oh_d = din("oh", [32, 8448])
    eye_d = din("eye_b", [128, 128], BF16)
    revi_d = din("revi_b", [128, 128], BF16)
    lt_d = din("lt_b", [128, 128], BF16)
    onesb_d = din("ones_b", [1, 128], BF16)
    cos_d = din("cos16", [128, NT * 512], BF16)
    sin_d = din("sin16", [128, NT * 512], BF16)

    y_d = nc.dram_tensor("y", [RPC, D], F32, kind="ExternalOutput").ap()
    dbg_d = nc.dram_tensor("dbg", [128, 16 * 1024], F32, kind="ExternalOutput").ap()

    snd1 = nc.dram_tensor("snd1", [NCORE, SH1], BF16, kind="Internal")
    rcv1 = nc.dram_tensor("rcv1", [NCORE, SH1], BF16, kind="Internal")
    snd2 = nc.dram_tensor("snd2", [NCORE, SH2], BF16, kind="Internal")
    rcv2 = nc.dram_tensor("rcv2", [NCORE, SH2], BF16, kind="Internal")
    tpd_d = nc.dram_tensor("tpd", [2, 8448], BF16, kind="Internal")
    w1q_d = nc.dram_tensor("w1q8", [D, FF], F8, kind="Internal")
    w2q_d = nc.dram_tensor("w2q8", [FF, D], F8, kind="Internal")

    from contextlib import ExitStack
    with ExitStack() as st:
        E = st.enter_context
        block = E(nc.Block())

        ps_m = E(nc.psum_tensor("ps_m", [128, 1536], F32))
        ps_a = E(nc.psum_tensor("ps_a", [128, 1536], F32))
        ps_c = E(nc.psum_tensor("ps_c", [128, 512], F32))
        ps_t = E(nc.psum_tensor("ps_t", [128, 1024], BF16))

        dbgst_s = E(nc.sbuf_tensor("dbgst_s", [128, 1024], F32))
        eye_s = E(nc.sbuf_tensor("eye_s", [128, 128], BF16))
        revi_s = E(nc.sbuf_tensor("revi_s", [128, 128], BF16))
        lt_s = E(nc.sbuf_tensor("lt_s", [128, 128], BF16))
        ones_s = E(nc.sbuf_tensor("ones_s", [1, 128], BF16))
        onesf_s = E(nc.sbuf_tensor("onesf_s", [128, 1], F32))
        onesfr_s = E(nc.sbuf_tensor("onesfr_s", [1, 128], F32))
        shft_s = E(nc.sbuf_tensor("shft_s", [128, 1], F32))
        b1_s = E(nc.sbuf_tensor("b1_s", [128, 32], F32))
        brow_s = E(nc.sbuf_tensor("brow_s", [1, 6 * D], BF16))
        rg_s = E(nc.sbuf_tensor("rg_s", [128, 2], F32))
        gsc_s = E(nc.sbuf_tensor("gsc_s", [1, 8], F32))
        mv_s = E(nc.sbuf_tensor("mv_s", [128, 3 * NT], F32))
        rstd_s = E(nc.sbuf_tensor("rstd_s", [128, NT], F32))
        ngm_s = E(nc.sbuf_tensor("ngm_s", [128, NT], F32))
        bn6_s = E(nc.sbuf_tensor("bn6_s", [128, 12], F32))
        stsc_s = E(nc.sbuf_tensor("stsc_s", [128, 1024], BF16))
        mn_s = E(nc.sbuf_tensor("mn_s", [128, NT], F32))
        eps_s = E(nc.sbuf_tensor("eps_s", [128, 1], F32))

        sem_names = ["ld", "wld", "wl2", "wl3", "wl4", "prj", "ev", "rp", "tr",
                     "ev2", "snd", "cc1", "cc2", "sc", "xp", "pm", "cx", "nm",
                     "qz", "gm", "ph", "rcv", "vv", "ss", "yo", "tp", "gt",
                     "bg", "hx", "vx", "lx0", "lx1", "soh", "swg", "svs",
                     "sq0", "sq1", "stp", "sw8", "sct", "sc2", "lf"]
        semh, semv = {}, {}
        for n in sem_names:
            semh[n] = E(nc.semaphore(n))
            semv[n] = 0

        P = Plan()

        def dma(eng, out, in_, sem):
            semv[sem] += 16
            def f(e, out=out, in_=in_, h=semh[sem]):
                e.dma_start(out=out, in_=in_).then_inc(h, 16)
            P.add(eng, f)
            return semv[sem]

        def wait(eng, sem, val):
            if val <= 0:
                return
            def f(e, h=semh[sem], val=val):
                e.wait_ge(h, val)
            P.add(eng, f)

        def emit(eng, fn, sem=None):
            val = None
            if sem is not None:
                semv[sem] += 1
                val = semv[sem]
            def f(e, fn=fn, sem=sem):
                r = fn(e)
                if sem is not None:
                    r.then_inc(semh[sem], 1)
            P.add(eng, f)
            return val

        dbgstate = {"n": 0}
        def tap(src_ap, width, via="vector"):
            if dbgstate["n"] >= DBGTAPS:
                dbgstate["n"] += 1
                return
            slot = dbgstate["n"]; dbgstate["n"] += 1
            wait(via, "yo", semv["yo"])
            def cp(e, src_ap=src_ap):
                return e.tensor_copy(dbgst_s[:, 0:width], src_ap)
            v = emit(via, cp, "vv")
            wait("sync", "vv", v)
            dma("sync", mkap(dbg_d, [[16 * 1024, 128], [1, width]], slot * 1024), dbgst_s[:, 0:width], "yo")

        def barrier():
            target = semv["ph"] + 5
            semv["ph"] = target
            for eng in Plan.ENGS:
                def f(e, t=target, h=semh["ph"]):
                    e.nop().then_inc(h, 1)
                    e.wait_ge(h, t)
                P.add(eng, f)

        def ln_finish():
            sumv = mkap(mv_s[:, :], [[3 * NT, 128], [2, NT]], 0)
            sqv = mkap(mv_s[:, :], [[3 * NT, 128], [2, NT]], 1)
            l1 = emit("vector", lambda e: e.tensor_scalar(
                mn_s[:, :], sumv, 1.0 / D, None, ALU.mult), "lf")
            emit("vector", lambda e: e.tensor_scalar(
                mv_s[:, 2 * NT:3 * NT], sqv, 1.0 / D, None, ALU.mult), "lf")
            wait("vector", "lf", l1)
            l3 = emit("vector", lambda e: e.tensor_tensor(
                ngm_s[:, :], mn_s[:, :], mn_s[:, :], ALU.mult), "lf")
            wait("vector", "lf", l3)
            vv1 = emit("vector", lambda e: e.tensor_tensor(
                mv_s[:, 2 * NT:3 * NT], mv_s[:, 2 * NT:3 * NT], ngm_s[:, :],
                ALU.subtract), "vv")
            wait("scalar", "vv", vv1)
            s1 = emit("scalar", lambda e: e.activation(
                mv_s[:, 2 * NT:3 * NT], mv_s[:, 2 * NT:3 * NT], AF.Ln,
                bias=eps_s[:, :]), "lf")
            wait("scalar", "lf", s1)
            sv = emit("scalar", lambda e: e.activation(
                rstd_s[:, :], mv_s[:, 2 * NT:3 * NT], AF.Exp, bias=0.0, scale=-0.5), "ss")
            wait("vector", "ss", sv)
            return emit("vector", lambda e: e.scalar_tensor_tensor(
                ngm_s[:, :], mn_s[:, :], -1.0, rstd_s[:, :], ALU.mult, ALU.mult), "vv")

        dma("sync", eye_s[:, :], eye_d, "ld")
        dma("sync", revi_s[:, :], revi_d, "ld")
        dma("sync", lt_s[:, :], lt_d, "ld")
        dma("sync", ones_s[:, :], onesb_d, "ld")
        dma("sync", b1_s[:, :], b1c_d, "ld")
        v_cl = dma("sync", brow_s[:, :], brows_d, "ld")
        emit("vector", lambda e: e.memset(onesf_s[:, :], 1.0))
        emit("vector", lambda e: e.memset(shft_s[:, :], EXP_SHIFT))
        emit("vector", lambda e: e.memset(eps_s[:, :], EPS))
        v_ones = emit("vector", lambda e: e.memset(onesfr_s[:, :], 1.0), "vv")

        with ExitStack() as p0:
            E0 = p0.enter_context
            xt = [E0(nc.sbuf_tensor(f"xt{i}", [128, D], F32)) for i in range(2)]
            wqkv_s = E0(nc.sbuf_tensor("wqkv_s", [128, 24 * D], BF16))
            xlt_t = E0(nc.sbuf_tensor("xlt_t", [128, D], BF16))
            xl_s = E0(nc.sbuf_tensor("xl_s", [128, D], BF16))
            qraw_s = E0(nc.sbuf_tensor("qraw_s", [128, 3 * D], BF16))
            qrp_s = E0(nc.sbuf_tensor("qrp_s", [128, 2 * D], BF16))
            rsc_s = E0(nc.sbuf_tensor("rsc_s", [128, 512], BF16))
            qtv_s = [E0(nc.sbuf_tensor(f"qtv{i}", [128, D], BF16)) for i in range(2)]
            cost_s = E0(nc.sbuf_tensor("cost_s", [128, 2 * 512], BF16))
            sint_s = E0(nc.sbuf_tensor("sint_s", [128, 2 * 512], BF16))
            wch = [E0(nc.sbuf_tensor(f"wch{i}", [128, 2048], F32)) for i in range(2)]
            wq8_s = E0(nc.sbuf_tensor("wq8_s", [128, 2048], F8))
            gp_s = E0(nc.sbuf_tensor("gp_s", [128, 52], F32))
            ohc_s = [E0(nc.sbuf_tensor(f"ohc{i}", [32, 512], F32)) for i in range(2)]
            rel2_s = E0(nc.sbuf_tensor("rel2_s", [32, 2], F32))
            tpc_s = E0(nc.sbuf_tensor("tpc_s", [2, 512], BF16))

            v_p0c = dma("sync", rel2_s[:, :], rel2_d, "ld")
            for wi, wd in enumerate([wq_d, wk_d, wv_d]):
                v_wqkv = dma("gpsimd",
                             mkap(wqkv_s[:, :], [[24 * D, 128], [D, 8], [1, D]], wi * 8 * D),
                             mkap(wd, [[D, 128], [128 * D, 8], [1, D]]), "wld")

            stat_done = [0] * NT
            for i in range(NT):
                bsel = i % 2
                if i >= 2:
                    wait("sync", "vv", stat_done[i - 2])
                v = dma("sync", xt[bsel][:, :], mkap(x_in, [[D, 128], [1, D]], i * 128 * D),
                        "lx0" if bsel == 0 else "lx1")
                wait("vector", "lx0" if bsel == 0 else "lx1", v)
                emit("vector", lambda e, b=bsel, i=i: e.tensor_scalar(
                    stsc_s[:, :], xt[b][:, :], 1.0, None, ALU.mult, ALU.add,
                    accum_out=mv_s[:, 2 * i:2 * i + 1]))
                stat_done[i] = emit("vector", lambda e, b=bsel, i=i: e.scalar_tensor_tensor(
                    stsc_s[:, :], xt[b][:, :], 1.0, xt[b][:, :], ALU.mult, ALU.mult,
                    accum_out=mv_s[:, 2 * i + 1:2 * i + 2]), "vv")
            v_rstd = ln_finish()

            wait("tensor", "wld", v_wqkv)
            wait("tensor", "ld", semv["ld"])
            apply_done = [0] * NT
            vsend_prev = 0
            ev_prev = 0
            rp_prev = 0
            trx_prev = 0
            prj_prev = 0
            qsend = [0, 0]
            rope_done = [0] * NT
            for i in range(NT):
                bsel = i % 2
                if i >= 2:
                    wait("sync", "ss", apply_done[i - 2])
                    wait("sync", "rp", rope_done[i - 2])
                else:
                    wait("sync", "vv", stat_done[NT - 2 + i])
                lxp = "lx0" if bsel == 0 else "lx1"
                dma("sync", xt[bsel][:, :], mkap(x_in, [[D, 128], [1, D]], i * 128 * D), lxp)
                dma("sync", mkap(cost_s[:, :], [[1024, 128], [1, 512]], bsel * 512),
                    mkap(cos_d, [[NT * 512, 128], [1, 512]], i * 512), lxp)
                vc = dma("sync", mkap(sint_s[:, :], [[1024, 128], [1, 512]], bsel * 512),
                         mkap(sin_d, [[NT * 512, 128], [1, 512]], i * 512), lxp)
                wait("scalar", lxp, vc)
                if i == 0:
                    wait("scalar", "vv", v_rstd)
                else:
                    wait("scalar", "tr", trx_prev)
                apply_done[i] = emit("scalar", lambda e, i=i, b=bsel: e.activation(
                    xl_s[:, :], xt[b][:, :], AF.Identity,
                    bias=ngm_s[:, i:i + 1], scale=rstd_s[:, i:i + 1]), "ss")
                wait("tensor", "ss", apply_done[i])
                wait("tensor", "ev2", semv["ev2"])
                def trx(e):
                    for d8 in range(8):
                        r = e.transpose(ps_t[:, d8 * 128:(d8 + 1) * 128],
                                        xl_s[:, d8 * 128:(d8 + 1) * 128], eye_s[:, :])
                    return r
                trx_prev = emit("tensor", trx, "tr")
                wait("vector", "tr", trx_prev)
                wait("vector", "prj", prj_prev)
                ev2a = emit("vector", lambda e: e.tensor_copy(xlt_t[:, :], ps_t[:, :]), "ev2")
                wait("tensor", "ev2", ev2a)
                wait("tensor", "ev", ev_prev)
                def prj(e):
                    for d8 in range(8):
                        lhsT = xlt_t[:, d8 * 128:(d8 + 1) * 128]
                        first, last = (d8 == 0), (d8 == 7)
                        for ch in range(2):
                            e.matmul(ps_m[:, ch * 512:(ch + 1) * 512], lhsT,
                                     wqkv_s[:, d8 * D + ch * 512:d8 * D + (ch + 1) * 512],
                                     start=first, stop=last)
                            e.matmul(ps_a[:, ch * 512:(ch + 1) * 512], lhsT,
                                     wqkv_s[:, 8 * D + d8 * D + ch * 512:8 * D + d8 * D + (ch + 1) * 512],
                                     start=first, stop=last)
                            dstv = ps_m[:, 1024:1536] if ch == 0 else ps_a[:, 1024:1536]
                            r = e.matmul(dstv, lhsT,
                                         wqkv_s[:, 16 * D + d8 * D + ch * 512:16 * D + d8 * D + (ch + 1) * 512],
                                         start=first, stop=last)
                    return r
                t_prj = emit("tensor", prj, "prj")
                prj_prev = t_prj
                wait("scalar", "prj", t_prj)
                wait("scalar", "rp", rp_prev)
                wait("scalar", "svs", vsend_prev)
                emit("scalar", lambda e: e.copy(qraw_s[:, 0:1024], ps_m[:, 0:1024]))
                emit("scalar", lambda e: e.copy(qraw_s[:, 1024:2048], ps_a[:, 0:1024]))
                emit("scalar", lambda e: e.copy(qraw_s[:, 2048:2560], ps_m[:, 1024:1536]))
                s_ev = emit("scalar", lambda e: e.copy(qraw_s[:, 2560:3072], ps_a[:, 1024:1536]), "ev")
                ev_prev = s_ev
                wait("vector", "ev", s_ev)
                wait("vector", lxp, vc)
                def rope(e, bsel=bsel):
                    for (soff, doff) in ((0, 0), (1024, 1024)):
                        v1 = mkap(qraw_s[:, :], [[3 * D, 128], [64, 16], [1, 32]], soff)
                        v2 = mkap(qraw_s[:, :], [[3 * D, 128], [64, 16], [1, 32]], soff + 32)
                        o1 = mkap(qrp_s[:, :], [[2 * D, 128], [64, 16], [1, 32]], doff)
                        o2 = mkap(qrp_s[:, :], [[2 * D, 128], [64, 16], [1, 32]], doff + 32)
                        csv = mkap(cost_s[:, :], [[1024, 128], [32, 16], [1, 32]], bsel * 512)
                        snv = mkap(sint_s[:, :], [[1024, 128], [32, 16], [1, 32]], bsel * 512)
                        scv = mkap(rsc_s[:, :], [[512, 128], [32, 16], [1, 32]], 0)
                        e.tensor_tensor(o1, v1, csv, ALU.mult)
                        e.tensor_tensor(o2, v2, csv, ALU.mult)
                        e.tensor_tensor(scv, v2, snv, ALU.mult)
                        e.tensor_tensor(o1, o1, scv, ALU.subtract)
                        e.tensor_tensor(scv, v1, snv, ALU.mult)
                        r = e.tensor_tensor(o2, o2, scv, ALU.add)
                    return r
                rp_prev = emit("vector", rope, "rp")
                rope_done[i] = rp_prev
                if i == 0:
                    tap(rstd_s[:, :], NT)
                    tap(ngm_s[:, :], NT)
                    tap(mv_s[:, :], 3 * NT)
                if False and i == NT - 1:
                    tap(qraw_s[:, 0:1024], 1024)
                    tap(qrp_s[:, 0:1024], 1024)
                wait("tensor", "rp", rp_prev)
                wait("tensor", "ev2", semv["ev2"])
                def trq(e):
                    for d8 in range(8):
                        r = e.transpose(ps_t[:, d8 * 128:(d8 + 1) * 128],
                                        qrp_s[:, d8 * 128:(d8 + 1) * 128], eye_s[:, :])
                    return r
                t_q = emit("tensor", trq, "tr")
                wait("vector", "tr", t_q)
                wait("vector", "sq0", qsend[0])
                evq = emit("vector", lambda e: e.tensor_copy(qtv_s[0][:, :], ps_t[:, :]), "ev2")
                wait("tensor", "ev2", evq)
                def trk(e):
                    for d8 in range(8):
                        r = e.transpose(ps_t[:, d8 * 128:(d8 + 1) * 128],
                                        qrp_s[:, 1024 + d8 * 128:1024 + (d8 + 1) * 128], eye_s[:, :])
                    return r
                t_k = emit("tensor", trk, "tr")
                wait("vector", "tr", t_k)
                wait("vector", "sq1", qsend[1])
                evk = emit("vector", lambda e: e.tensor_copy(qtv_s[1][:, :], ps_t[:, :]), "ev2")
                wait("sync", "ev2", evq)
                qsend[0] = dma("sync",
                               mkap(snd1.ap(), [[2048, 128], [SH1, 8], [1, 128]], i * 128),
                               mkap(qtv_s[0][:, :], [[D, 128], [128, 8], [1, 128]], 0), "sq0")
                wait("sync", "ev2", evk)
                qsend[1] = dma("sync",
                               mkap(snd1.ap(), [[2048, 128], [SH1, 8], [1, 128]],
                                    128 * RPC + i * 128),
                               mkap(qtv_s[1][:, :], [[D, 128], [128, 8], [1, 128]], 0), "sq1")
                wait("sync", "ev", s_ev)
                vsend_prev = dma("sync",
                    mkap(snd1.ap(), [[128, 128], [SH1, 8], [1, 128]],
                         2 * 128 * RPC + i * 128 * 128),
                    mkap(qraw_s[:, :], [[3 * D, 128], [128, 8], [1, 128]], 2048),
                    "svs")


            tpev = 0
            for ch in range(17):
                w = 512 if ch < 16 else 256
                bsel = ch % 2
                if ch >= 2:
                    wait("sync", "sc", semv["sc"])
                v_oh = dma("sync", ohc_s[bsel][:, 0:w],
                           mkap(oh_d, [[8448, 32], [1, w]], ch * 512), "soh")
                wait("tensor", "soh", v_oh)
                wait("tensor", "tp", tpev)
                t_v = emit("tensor", lambda e, w=w, b=bsel: e.matmul(
                    ps_c[0:2, 0:w], rel2_s[:, :], ohc_s[b][:, 0:w],
                    start=True, stop=True), "sc")
                wait("scalar", "sc", t_v)
                wait("scalar", "stp", semv["stp"])
                tpev = emit("scalar", lambda e, w=w: e.copy(tpc_s[:, 0:w], ps_c[0:2, 0:w]), "tp")
                wait("sync", "tp", tpev)
                dma("sync", mkap(tpd_d.ap(), [[8448, 2], [1, w]], ch * 512),
                    tpc_s[:, 0:w], "stp")

            CH1 = [(r * 128 * FF + cc * 2048, 2048, 0) for r in range(8) for cc in range(2)]
            CH2 = [(r * 128 * D, D, 1) for r in range(32)]
            CHUNKS = CH1 + CH2
            red_done = [0] * 48
            for gidx, (off, ncols, wi) in enumerate(CHUNKS):
                wd = w1t_d if wi == 0 else w2t_d
                ld_stride = FF if wi == 0 else D
                bsel = gidx % 2
                if gidx >= 2:
                    wait("sync", "vv", red_done[gidx - 2])
                v = dma("sync", wch[bsel][:, 0:ncols],
                        mkap(wd, [[ld_stride, 128], [1, ncols]], off), "swg")
                wait("vector", "swg", v)
                red_done[gidx] = emit(
                    "vector", lambda e, b=bsel, gidx=gidx, ncols=ncols: e.tensor_reduce(
                        gp_s[:, gidx:gidx + 1], wch[b][:, 0:ncols],
                        AX.X, ALU.add, apply_absolute_value=True), "vv")
            wait("vector", "vv", red_done[47])
            emit("vector", lambda e: e.tensor_reduce(gp_s[:, 48:49], gp_s[:, 0:16], AX.X, ALU.add))
            vg2 = emit("vector", lambda e: e.tensor_reduce(gp_s[:, 49:50], gp_s[:, 16:48], AX.X, ALU.add), "vv")
            wait("tensor", "vv", vg2)
            wait("tensor", "tp", semv["tp"])
            for wi in range(2):
                emit("tensor", lambda e, wi=wi: e.matmul(
                    ps_c[0:1, 128 + wi:129 + wi], gp_s[:, 48 + wi:49 + wi],
                    onesf_s[:, :], start=True, stop=True), "gm")
            wait("vector", "gm", semv["gm"])
            v_g1 = emit("vector", lambda e: e.tensor_scalar(
                gsc_s[:, 0:2], ps_c[0:1, 128:130], 1.0 / (D * FF), EPS, ALU.mult, ALU.add), "lf")
            wait("vector", "lf", v_g1)
            vg3 = emit("vector", lambda e: e.reciprocal(gsc_s[:, 2:4], gsc_s[:, 0:2]), "vv")
            wait("tensor", "vv", vg3)
            emit("tensor", lambda e: e.matmul(
                ps_c[:, 384:386], onesfr_s[:, :], gsc_s[:, 2:4], start=True, stop=True), "gm")
            wait("vector", "gm", semv["gm"])
            v_rg = emit("vector", lambda e: e.tensor_copy(rg_s[:, 0:2], ps_c[:, 384:386]), "vv")

            qz_done = [0] * 48
            for gidx, (off, ncols, wi) in enumerate(CHUNKS):
                wd = w1t_d if wi == 0 else w2t_d
                wqd = w1q_d if wi == 0 else w2q_d
                ld_stride = FF if wi == 0 else D
                bsel = gidx % 2
                if gidx < 2:
                    wait("sync", "vv", v_rg)
                else:
                    wait("sync", "qz", qz_done[gidx - 2])
                v = dma("sync", wch[bsel][:, 0:ncols],
                        mkap(wd, [[ld_stride, 128], [1, ncols]], off), "swg")
                wait("vector", "swg", v)
                if gidx == 0:
                    wait("vector", "vv", v_rg)
                wait("vector", "sw8", semv["sw8"])
                q1 = emit("vector", lambda e, b=bsel, wi=wi, ncols=ncols: e.tensor_scalar(
                    wch[b][:, 0:ncols], wch[b][:, 0:ncols],
                    rg_s[:, wi:wi + 1], MAGIC, ALU.mult, ALU.add), "lf")
                wait("vector", "lf", q1)
                q2 = emit("vector", lambda e, b=bsel, ncols=ncols: e.tensor_scalar(
                    wch[b][:, 0:ncols], wch[b][:, 0:ncols],
                    MAGIC, 1.0, ALU.subtract, ALU.min), "lf")
                wait("vector", "lf", q2)
                qz_done[gidx] = emit("vector", lambda e, b=bsel, ncols=ncols: e.tensor_scalar(
                    wq8_s[:, 0:ncols], wch[b][:, 0:ncols],
                    -1.0, None, ALU.max), "qz")
                wait("sync", "qz", qz_done[gidx])
                dma("sync", mkap(wqd.ap(), [[ld_stride, 128], [1, ncols]], off),
                    wq8_s[:, 0:ncols], "sw8")
            for snm in ("svs", "sq0", "sq1", "stp", "sw8"):
                wait("gpsimd", snm, semv[snm])
            def cc1f(e):
                if NOCC:
                    return e.nop()
                with allow_ici_all_to_all():
                    return e.collective_compute(
                        "AllToAll", ALU.bypass, replica_groups=[list(range(8))],
                        ins=[snd1.ap().opt()], outs=[rcv1.ap().opt()])
            emit("gpsimd", cc1f, "cc1")

        barrier()

        if phases >= 2:
            with ExitStack() as pa:
                EA = pa.enter_context
                qt2_s = [EA(nc.sbuf_tensor(f"qt2_{b}", [128, S], BF16)) for b in range(2)]
                kt2_s = [EA(nc.sbuf_tensor(f"kt2_{b}", [128, S], BF16)) for b in range(2)]
                vau_s = [EA(nc.sbuf_tensor(f"vau_{b}", [128, 64 * 130], BF16)) for b in range(2)]
                gt_s = EA(nc.sbuf_tensor("gt_s", [128, 8192], BF16))
                eb_s = EA(nc.sbuf_tensor("eb_s", [128, 67 * 128], BF16))
                exp_s = [EA(nc.sbuf_tensor(f"exp_{i}", [128, 1536], BF16)) for i in range(2)]
                prb_s = [EA(nc.sbuf_tensor(f"prb_{i}", [128, 1536], BF16)) for i in range(2)]
                ctx_s = EA(nc.sbuf_tensor("ctx_s", [128, 4 * 64], BF16))
                rsum_s = EA(nc.sbuf_tensor("rsum_s", [128, 4], F32))

                wait("sync", "cc1", 1)
                for b in range(2):
                    for sb in range(4):
                        j = 4 * b + sb
                        dma("sync", qt2_s[b][:, sb * 2048:(sb + 1) * 2048],
                            mkap(rcv1.ap(), [[2048, 128], [1, 2048]], j * SH1), "rcv")
                        dma("sync", kt2_s[b][:, sb * 2048:(sb + 1) * 2048],
                            mkap(rcv1.ap(), [[2048, 128], [1, 2048]], j * SH1 + 128 * RPC), "rcv")
                for b in range(2):
                    emit("vector", lambda e, b=b: e.memset(vau_s[b][:, :], 1.0), "vv")
                v_vau0 = semv["vv"]
                wait("sync", "vv", v_vau0)
                for b in range(2):
                    for sb in range(4):
                        j = 4 * b + sb
                        for hl2 in range(2):
                            dma("sync",
                                mkap(vau_s[b][:, :], [[64 * 130, 128], [130, 16], [1, 64]],
                                     sb * 16 * 130 + 65 * hl2),
                                mkap(rcv1.ap(), [[128, 128], [128 * 128, 16], [1, 64]],
                                     j * SH1 + 2 * 128 * RPC + 64 * hl2),
                                "rcv")
                v_rcv = semv["rcv"]
                wait("tensor", "rcv", v_rcv)
                v_eb0 = emit("vector", lambda e: e.memset(eb_s[:, 64 * 128:67 * 128], 0.0), "vv")
                wait("vector", "rcv", v_rcv)
                tap(qt2_s[0][:, 0:1024], 1024)
                tap(vau_s[0][:, 0:1024], 1024)
                wait("tensor", "vv", v_eb0)

                bank_last_xp = [0, 0]
                expbuf_last_pm = [0, 0]
                prbbuf_last_cx = [0, 0]
                for hl in range(2):
                    v_gt = dma("gpsimd", gt_s[:, :],
                               mkap(tpd_d.ap(), [[1, 128], [1, 8192]], hl * 8448), "gt")
                    wait("tensor", "gt", v_gt)
                    wait("tensor", "nm", semv["nm"])
                    bg_base = semv["bg"]
                    for o in range(64):
                        if o >= 2:
                            wait("tensor", "bg", bg_base + o - 1)
                        psB = ps_c if (o % 2 == 0) else ps_a
                        t_o = emit("tensor", lambda e, o=o, psB=psB: e.matmul(
                            psB[:, 0:128], revi_s[:, :],
                            gt_s[:, o * 128:(o + 1) * 128], start=True, stop=True), "sc")
                        wait("scalar", "sc", t_o)
                        emit("scalar", lambda e, o=o, psB=psB: e.activation(
                            eb_s[:, (63 - o) * 128:(64 - o) * 128],
                            psB[:, 0:128], AF.Exp), "bg")
                    wait("vector", "bg", semv["bg"])
                    v_msk = emit("vector", lambda e: e.tensor_tensor(
                        eb_s[:, 63 * 128:64 * 128], eb_s[:, 63 * 128:64 * 128],
                        lt_s[:, :], ALU.mult), "vv")
                    for eng in ("tensor", "scalar", "gpsimd"):
                        wait(eng, "vv", v_msk)
                    if hl == 0:
                        tap(eb_s[:, 62 * 128:63 * 128], 128)
                        tap(gt_s[:, 0:1024], 1024, via="vector")

                    for b in range(2):
                        base = 64 * hl
                        for qc in range(NQC):
                            nk = 4 * qc + 4
                            groups = [(g0, min(3, nk - g0)) for g0 in range(0, nk, 3)]
                            for gi, (g0, gw) in enumerate(groups):
                                sel = gi % 2
                                psS = ps_m if sel == 0 else ps_a
                                wait("tensor", "xp", bank_last_xp[sel])
                                def scores(e, g0=g0, gw=gw, psS=psS, b=b, base=base, qc=qc):
                                    for m in range(gw):
                                        kt = g0 + m
                                        r = e.matmul(
                                            psS[:, m * 512:(m + 1) * 512],
                                            kt2_s[b][base:base + 64, kt * 128:(kt + 1) * 128],
                                            qt2_s[b][base:base + 64, qc * 512:(qc + 1) * 512],
                                            start=True, stop=True)
                                    return r
                                t_sc = emit("tensor", scores, "sc")
                                wait("scalar", "sc", t_sc)
                                wait("scalar", "pm", expbuf_last_pm[sel])
                                s_xp = emit("scalar", lambda e, gw=gw, sel=sel, psS=psS: e.activation(
                                    exp_s[sel][:, 0:gw * 512], psS[:, 0:gw * 512], AF.Exp,
                                    bias=shft_s[:, :], scale=0.125), "xp")
                                bank_last_xp[sel] = s_xp
                                wait("vector", "xp", s_xp)
                                wait("vector", "cx", prbbuf_last_cx[sel])
                                def probs(e, g0=g0, gw=gw, sel=sel, qc=qc):
                                    bias = mkap(eb_s[:, :],
                                                [[67 * 128, 128], [128, gw], [-128, 4], [1, 128]],
                                                (63 - 4 * qc + g0) * 128)
                                    src = mkap(exp_s[sel][:, :],
                                               [[1536, 128], [512, gw], [128, 4], [1, 128]], 0)
                                    dst = mkap(prb_s[sel][:, :],
                                               [[1536, 128], [512, gw], [128, 4], [1, 128]], 0)
                                    return e.tensor_tensor(dst, src, bias, ALU.mult)
                                v_pm = emit("vector", probs, "pm")
                                expbuf_last_pm[sel] = v_pm
                                wait("tensor", "pm", v_pm)
                                if gi == 0:
                                    wait("tensor", "nm", semv["nm"])
                                def ctxmm(e, g0=g0, gw=gw, sel=sel, qc=qc, b=b, hl=hl):
                                    r = None
                                    for m in range(gw):
                                        kt = g0 + m
                                        for j in range(4):
                                            qt = 4 * qc + j
                                            if qt < kt:
                                                continue
                                            r = e.matmul(
                                                ps_c[:, j * 66:j * 66 + 65],
                                                prb_s[sel][:, m * 512 + j * 128:m * 512 + (j + 1) * 128],
                                                vau_s[b][:, kt * 130 + 65 * hl:kt * 130 + 65 * hl + 65],
                                                start=(kt == 0 and j == 0), stop=(kt == qt),
                                                skip_group_check=True)
                                    return r
                                t_cx = emit("tensor", ctxmm, "cx")
                                prbbuf_last_cx[sel] = t_cx
                            wait("vector", "cx", semv["cx"])
                            wait("vector", "sct", semv["sct"])
                            v_rc = emit("vector", lambda e: e.reciprocal(
                                rsum_s[:, 0:4], mkap(ps_c[:, :], [[512, 128], [66, 4]], 64)), "lf")
                            wait("vector", "lf", v_rc)
                            def norm(e):
                                for j in range(4):
                                    r = e.tensor_scalar(ctx_s[:, j * 64:(j + 1) * 64],
                                                        ps_c[:, j * 66:j * 66 + 64],
                                                        rsum_s[:, j:j + 1], None, ALU.mult)
                                return r
                            v_nm = emit("vector", norm, "nm")
                            wait("sync", "nm", v_nm)
                            jdst = 4 * b + (qc * 512) // 2048
                            ro = (qc * 512) % 2048
                            dma("sync",
                                mkap(snd2.ap(), [[128, 128], [128 * 128, 4], [1, 64]],
                                     jdst * SH2 + ro * 128 + hl * 64),
                                mkap(ctx_s[:, :], [[256, 128], [64, 4], [1, 64]], 0),
                                "sct")
                v_snd2 = semv["sct"]
                wait("gpsimd", "sct", v_snd2)
                def cc2f(e):
                    with allow_ici_all_to_all():
                        return e.collective_compute(
                            "AllToAll", ALU.bypass, replica_groups=[list(range(8))],
                            ins=[snd2.ap().opt()], outs=[rcv2.ap().opt()])
                emit("gpsimd", cc2f, "cc2")

        barrier()

        if phases >= 3:
            with ExitStack() as pc:
                EC = pc.enter_context
                x_sb = EC(nc.sbuf_tensor("x_sb", [128, NT * D], F32))

                with ExitStack() as pc0:
                    E1 = pc0.enter_context
                    wo_s = E1(nc.sbuf_tensor("wo_s", [128, 8 * D], BF16))
                    cstg = E1(nc.sbuf_tensor("cstg", [128, 1024], BF16))
                    ctxt_s = E1(nc.sbuf_tensor("ctxt_s", [128, 1024], BF16))
                    xt0 = [E1(nc.sbuf_tensor(f"xt0_{i}", [128, D], F32)) for i in range(2)]

                    v_wo = dma("gpsimd", mkap(wo_s[:, :], [[8 * D, 128], [D, 8], [1, D]]),
                               mkap(wo_d, [[D, 128], [128 * D, 8], [1, D]]), "wl2")
                    wait("sync", "cc2", 1)
                    wait("tensor", "wl2", v_wo)
                    res_done = [0] * NT
                    cst_done = [0] * NT
                    wo_last = 0
                    for i in range(NT):
                        bsel = i % 2
                        if i >= 1:
                            wait("sync", "tr", cst_done[i - 1])
                        v1 = dma("sync", cstg[:, :],
                                 mkap(rcv2.ap(), [[128, 128], [SH2, 8], [1, 128]], i * 128 * 128),
                                 "rcv")
                        if i >= 2:
                            wait("sync", "vv", res_done[i - 2])
                        v2 = dma("sync", xt0[bsel][:, :],
                                 mkap(x_in, [[D, 128], [1, D]], i * 128 * D), "lx0" if bsel == 0 else "lx1")
                        wait("tensor", "rcv", v1)
                        wait("tensor", "ev2", semv["ev2"])
                        def trc(e):
                            for d8 in range(8):
                                r = e.transpose(ps_t[:, d8 * 128:(d8 + 1) * 128],
                                                cstg[:, d8 * 128:(d8 + 1) * 128], eye_s[:, :])
                            return r
                        t_c = emit("tensor", trc, "tr")
                        cst_done[i] = t_c
                        wait("vector", "tr", t_c)
                        wait("vector", "prj", wo_last)
                        evc = emit("vector", lambda e: e.tensor_copy(ctxt_s[:, :], ps_t[:, :]), "ev2")
                        wait("tensor", "ev2", evc)
                        wait("tensor", "vv", semv["vv"])
                        def womm(e):
                            for d8 in range(8):
                                for ch in range(2):
                                    r = e.matmul(ps_m[:, ch * 512:(ch + 1) * 512],
                                                 ctxt_s[:, d8 * 128:(d8 + 1) * 128],
                                                 wo_s[:, d8 * D + ch * 512:d8 * D + (ch + 1) * 512],
                                                 start=(d8 == 0), stop=(d8 == 7))
                            return r
                        t_wo = emit("tensor", womm, "prj")
                        wo_last = t_wo
                        wait("vector", "prj", t_wo)
                        wait("vector", "lx0" if bsel == 0 else "lx1", v2)
                        res_done[i] = emit("vector", lambda e, i=i, b=bsel: e.tensor_tensor(
                            x_sb[:, i * D:(i + 1) * D], ps_m[:, 0:1024], xt0[b][:, :], ALU.add), "vv")

                tap(x_sb[:, 0:1024], 1024)
                def ln_stats_batch():
                    wait("vector", "ss", semv["ss"])
                    for i in range(NT):
                        emit("vector", lambda e, i=i: e.tensor_scalar(
                            stsc_s[:, :], x_sb[:, i * D:(i + 1) * D], 1.0, None,
                            ALU.mult, ALU.add, accum_out=mv_s[:, 2 * i:2 * i + 1]))
                        emit("vector", lambda e, i=i: e.scalar_tensor_tensor(
                            stsc_s[:, :], x_sb[:, i * D:(i + 1) * D], 1.0,
                            x_sb[:, i * D:(i + 1) * D], ALU.mult, ALU.mult,
                            accum_out=mv_s[:, 2 * i + 1:2 * i + 2]), "vv")
                    return ln_finish()

                def ffn_pass(tag):
                    with ExitStack() as pf:
                        E2 = pf.enter_context
                        w1q_s = E2(nc.sbuf_tensor(f"w1q_{tag}", [128, 8 * FF], F8))
                        w2q_s = E2(nc.sbuf_tensor(f"w2q_{tag}", [128, 32 * D], F8))
                        xl2 = E2(nc.sbuf_tensor(f"xl2_{tag}", [128, D], BF16))
                        xl2t = E2(nc.sbuf_tensor(f"xl2t_{tag}", [128, D], BF16))
                        ht_s = E2(nc.sbuf_tensor(f"ht_{tag}", [128, 32 * 128], BF16))
                        dma("sync", mkap(w1q_s[:, :], [[8 * FF, 128], [FF, 8], [1, FF]]),
                            mkap(w1q_d.ap(), [[FF, 128], [128 * FF, 8], [1, FF]]), "wl3")
                        v_w = dma("sync", mkap(w2q_s[:, :], [[32 * D, 128], [D, 32], [1, D]]),
                                  mkap(w2q_d.ap(), [[D, 128], [128 * D, 32], [1, D]]), "wl3")
                        v_ln = ln_stats_batch()
                        wait("scalar", "vv", v_ln)
                        wait("tensor", "wl3", v_w)
                        relu_last = {id(ps_m): 0, id(ps_a): 0}
                        trl_prev = 0
                        ffn1_last = 0
                        for i in range(NT):
                            wait("scalar", "tr", trl_prev)
                            s_l = emit("scalar", lambda e, i=i: e.activation(
                                xl2[:, :], x_sb[:, i * D:(i + 1) * D], AF.Identity,
                                bias=ngm_s[:, i:i + 1], scale=rstd_s[:, i:i + 1]), "ss")
                            if i == 0 and DBGTAPS > 8 and tag == "fa":
                                wait("vector", "ss", s_l)
                                tap(xl2[:, :], 1024)
                                tap(w1q_s[:, 0:1024], 1024)
                                tap(rstd_s[:, :], NT)
                                tap(mn_s[:, :], NT)
                            elif i == 0 and tag == "fa":
                                dbgstate["n"] += 4
                            wait("tensor", "ss", s_l)
                            wait("tensor", "ev2", semv["ev2"])
                            def trl(e, xl2=xl2):
                                for d8 in range(8):
                                    r = e.transpose(ps_t[:, d8 * 128:(d8 + 1) * 128],
                                                    xl2[:, d8 * 128:(d8 + 1) * 128], eye_s[:, :])
                                return r
                            trl_prev = emit("tensor", trl, "tr")
                            wait("vector", "tr", trl_prev)
                            wait("vector", "prj", ffn1_last)
                            ev_l = emit("vector", lambda e, xl2t=xl2t: e.tensor_copy(xl2t[:, :], ps_t[:, :]), "ev2")
                            wait("tensor", "ev2", ev_l)
                            for (ft0, ftn, psF) in ((0, 12, ps_m), (12, 12, ps_a), (24, 8, ps_m)):
                                wait("tensor", "hx", relu_last[id(psF)])
                                def ffn1b(e, ft0=ft0, ftn=ftn, psF=psF, w1q_s=w1q_s, xl2t=xl2t):
                                    for fl in range(ftn):
                                        ft = ft0 + fl
                                        for d8 in range(8):
                                            r = e.matmul(
                                                psF[:, fl * 128:(fl + 1) * 128],
                                                w1q_s[:, d8 * FF + ft * 128:d8 * FF + (ft + 1) * 128],
                                                xl2t[:, d8 * 128:(d8 + 1) * 128],
                                                start=(d8 == 0), stop=(d8 == 7))
                                    return r
                                t_f = emit("tensor", ffn1b, "prj")
                                ffn1_last = t_f
                                wait("scalar", "prj", t_f)
                                def relu(e, ft0=ft0, ftn=ftn, psF=psF, ht_s=ht_s):
                                    for fl in range(ftn):
                                        ft = ft0 + fl
                                        r = e.activation(ht_s[:, ft * 128:(ft + 1) * 128],
                                                         psF[:, fl * 128:(fl + 1) * 128],
                                                         AF.Relu, bias=b1_s[:, ft:ft + 1])
                                    return r
                                relu_last[id(psF)] = emit("scalar", relu, "hx")
                            s_h = semv["hx"]
                            if i == 0 and DBGTAPS > 12 and tag == "fa":
                                wait("vector", "hx", s_h)
                                tap(ht_s[:, 0:1024], 1024)
                                tap(w2q_s[:, 0:1024], 1024)
                            elif i == 0 and tag == "fa":
                                dbgstate["n"] += 2
                            wait("tensor", "hx", s_h)
                            for half in range(2):
                                wait("tensor", "vx", semv["vx"])
                                def ffn2(e, half=half, ht_s=ht_s, w2q_s=w2q_s):
                                    for ft in range(32):
                                        e.matmul(ps_c[:, :],
                                                 ht_s[:, ft * 128:(ft + 1) * 128],
                                                 w2q_s[:, ft * D + half * 512:ft * D + (half + 1) * 512],
                                                 start=(ft == 0), stop=False)
                                    return e.matmul(ps_c[:, :], ones_s[:, :],
                                                    brow_s[0:1, half * 512:(half + 1) * 512],
                                                    start=False, stop=True)
                                t_2 = emit("tensor", ffn2, "prj")
                                wait("vector", "prj", t_2)
                                emit("vector", lambda e, i=i, half=half: e.tensor_tensor(
                                    x_sb[:, i * D + half * 512:i * D + (half + 1) * 512],
                                    ps_c[:, :],
                                    x_sb[:, i * D + half * 512:i * D + (half + 1) * 512],
                                    ALU.add), "vx")

                def attnb_pass():
                    with ExitStack() as pb:
                        E3 = pb.enter_context
                        wqb_s = E3(nc.sbuf_tensor("wqb_s", [128, 8 * D], BF16))
                        wdn_s = E3(nc.sbuf_tensor("wdn_s", [128, 8 * DL], BF16))
                        wup_s = E3(nc.sbuf_tensor("wup_s", [128, 2 * 2 * D], BF16))
                        wob_s = E3(nc.sbuf_tensor("wob_s", [128, 8 * D], BF16))
                        cos2 = E3(nc.sbuf_tensor("cos_s2", [128, NT * 512], BF16))
                        sin2 = E3(nc.sbuf_tensor("sin_s2", [128, NT * 512], BF16))
                        xl3 = E3(nc.sbuf_tensor("xl3", [128, D], BF16))
                        xl3t = E3(nc.sbuf_tensor("xl3t", [128, D], BF16))
                        q_sb = E3(nc.sbuf_tensor("q_sb", [128, D], BF16))
                        lat_sb = E3(nc.sbuf_tensor("lat_sb", [128, DL], BF16))
                        latt_sb = E3(nc.sbuf_tensor("latt_sb", [128, 2 * 128], BF16))
                        kv_sb = E3(nc.sbuf_tensor("kv_sb", [128, 2 * D], BF16))
                        k2_sb = E3(nc.sbuf_tensor("k2_sb", [128, 2 * D], BF16))
                        v2_sb = E3(nc.sbuf_tensor("v2_sb", [128, 2 * D], BF16))
                        att_sb = E3(nc.sbuf_tensor("att_sb", [128, 256], F32))
                        mx_sb = E3(nc.sbuf_tensor("mx_sb", [128, 16], F32))
                        sm_sb = E3(nc.sbuf_tensor("sm_sb", [128, 16], F32))
                        rc_sb = E3(nc.sbuf_tensor("rc_sb", [128, 16], F32))
                        prod_sb = E3(nc.sbuf_tensor("prod_sb", [128, D], BF16))
                        acc_sb = E3(nc.sbuf_tensor("acc_sb", [128, D], F32))
                        outb_sb = E3(nc.sbuf_tensor("outb_sb", [128, D], BF16))
                        obt_sb = E3(nc.sbuf_tensor("obt_sb", [128, D], BF16))

                        v_wb = dma("gpsimd", mkap(wqb_s[:, :], [[8 * D, 128], [D, 8], [1, D]]),
                            mkap(wqb_d, [[D, 128], [128 * D, 8], [1, D]]), "wl4")
                        wait("gpsimd", "wl4", v_wb)
                        v_wb = dma("gpsimd", mkap(wdn_s[:, :], [[8 * DL, 128], [DL, 8], [1, DL]]),
                            mkap(wdn_d, [[DL, 128], [128 * DL, 8], [1, DL]]), "wl4")
                        wait("gpsimd", "wl4", v_wb)
                        v_wb = dma("gpsimd", mkap(wup_s[:, :], [[4 * D, 128], [2 * D, 2], [1, 2 * D]]),
                            mkap(wup_d, [[2 * D, 128], [128 * 2 * D, 2], [1, 2 * D]]), "wl4")
                        wait("gpsimd", "wl4", v_wb)
                        v_wb = dma("gpsimd", mkap(wob_s[:, :], [[8 * D, 128], [D, 8], [1, D]]),
                                   mkap(wob_d, [[D, 128], [128 * D, 8], [1, D]]), "wl4")
                        dma("sync", cos2[:, :], cos_d, "sc2")
                        v_cs = dma("sync", sin2[:, :], sin_d, "sc2")
                        v_ln = ln_stats_batch()
                        wait("scalar", "vv", v_ln)
                        wait("tensor", "wl4", v_wb)
                        wait("vector", "sc2", v_cs)
                        trb_prev = 0
                        kv_last = 0
                        wob_last = 0
                        for i in range(NT):
                            wait("scalar", "tr", trb_prev)
                            s_l = emit("scalar", lambda e, i=i: e.activation(
                                xl3[:, :], x_sb[:, i * D:(i + 1) * D], AF.Identity,
                                bias=ngm_s[:, i:i + 1], scale=rstd_s[:, i:i + 1]), "ss")
                            wait("tensor", "ss", s_l)
                            wait("tensor", "ev2", semv["ev2"])
                            def trl(e):
                                for d8 in range(8):
                                    r = e.transpose(ps_t[:, d8 * 128:(d8 + 1) * 128],
                                                    xl3[:, d8 * 128:(d8 + 1) * 128], eye_s[:, :])
                                return r
                            trb_prev = emit("tensor", trl, "tr")
                            wait("vector", "tr", trb_prev)
                            wait("vector", "prj", kv_last)
                            ev_l = emit("vector", lambda e: e.tensor_copy(xl3t[:, :], ps_t[:, :]), "ev2")
                            wait("tensor", "ev2", ev_l)
                            wait("tensor", "vx", semv["vx"])
                            wait("tensor", "ev", semv["ev"])
                            def projq(e, xl3t=xl3t):
                                for d8 in range(8):
                                    for ch in range(2):
                                        e.matmul(ps_m[:, ch * 512:(ch + 1) * 512],
                                                 xl3t[:, d8 * 128:(d8 + 1) * 128],
                                                 wqb_s[:, d8 * D + ch * 512:d8 * D + (ch + 1) * 512],
                                                 start=(d8 == 0), stop=False)
                                    e.matmul(ps_c[:, 0:256],
                                             xl3t[:, d8 * 128:(d8 + 1) * 128],
                                             wdn_s[:, d8 * DL:(d8 + 1) * DL],
                                             start=(d8 == 0), stop=False)
                                e.matmul(ps_m[:, 0:512], ones_s[:, :], brow_s[0:1, D:D + 512],
                                         start=False, stop=True)
                                e.matmul(ps_m[:, 512:1024], ones_s[:, :], brow_s[0:1, D + 512:2 * D],
                                         start=False, stop=True)
                                return e.matmul(ps_c[:, 0:256], ones_s[:, :],
                                                brow_s[0:1, 2 * D:2 * D + 256],
                                                start=False, stop=True)
                            t_q = emit("tensor", projq, "prj")
                            wait("scalar", "prj", t_q)
                            wait("scalar", "rp", semv["rp"])
                            emit("scalar", lambda e: e.copy(q_sb[:, :], ps_m[:, 0:1024]))
                            s_lat = emit("scalar", lambda e: e.copy(lat_sb[:, :], ps_c[:, 0:256]), "ev")
                            wait("tensor", "ev", s_lat)
                            wait("tensor", "ev2", semv["ev2"])
                            def trlat(e):
                                e.transpose(ps_t[:, 0:128], lat_sb[:, 0:128], eye_s[:, :])
                                return e.transpose(ps_t[:, 128:256], lat_sb[:, 128:256], eye_s[:, :])
                            t_lt = emit("tensor", trlat, "tr")
                            wait("vector", "tr", t_lt)
                            wait("vector", "prj", kv_last)
                            ev_lt = emit("vector", lambda e: e.tensor_copy(latt_sb[:, :], ps_t[:, 0:256]), "ev2")
                            wait("tensor", "ev2", ev_lt)
                            def kvmm(e, latt_sb=latt_sb):
                                for ch in range(4):
                                    dst = ps_a[:, ch * 512:(ch + 1) * 512] if ch < 3 else ps_m[:, 0:512]
                                    for lt2 in range(2):
                                        e.matmul(dst,
                                                 latt_sb[:, lt2 * 128:(lt2 + 1) * 128],
                                                 wup_s[:, lt2 * 2 * D + ch * 512:lt2 * 2 * D + (ch + 1) * 512],
                                                 start=(lt2 == 0), stop=False)
                                    r = e.matmul(dst, ones_s[:, :],
                                                 brow_s[0:1, 3 * D + ch * 512:3 * D + (ch + 1) * 512],
                                                 start=False, stop=True)
                                return r
                            t_kv = emit("tensor", kvmm, "prj")
                            kv_last = t_kv
                            wait("scalar", "prj", t_kv)
                            emit("scalar", lambda e: e.copy(kv_sb[:, 0:1536], ps_a[:, 0:1536]))
                            s_kv = emit("scalar", lambda e: e.copy(kv_sb[:, 1536:2048], ps_m[:, 0:512]), "ev")
                            wait("vector", "ev", s_kv)
                            if i == 0:
                                tap(xl3[:, :], 1024)
                                tap(q_sb[:, :], 1024)
                                tap(kv_sb[:, 0:1024], 1024)
                                tap(kv_sb[:, 1024:2048], 1024)
                            def headmix(e, i=i):
                                csv = mkap(cos2[:, :], [[NT * 512, 128], [32, 16], [1, 32]], i * 512)
                                snv = mkap(sin2[:, :], [[NT * 512, 128], [32, 16], [1, 32]], i * 512)
                                qv1 = mkap(q_sb[:, :], [[D, 128], [64, 16], [1, 32]], 0)
                                qv2 = mkap(q_sb[:, :], [[D, 128], [64, 16], [1, 32]], 32)
                                p1 = mkap(prod_sb[:, :], [[D, 128], [64, 16], [1, 32]], 0)
                                p2 = mkap(prod_sb[:, :], [[D, 128], [64, 16], [1, 32]], 32)
                                sc = mkap(outb_sb[:, :], [[D, 128], [64, 16], [1, 32]], 0)
                                e.tensor_tensor(p1, qv1, csv, ALU.mult)
                                e.tensor_tensor(p2, qv2, csv, ALU.mult)
                                e.tensor_tensor(sc, qv2, snv, ALU.mult)
                                e.tensor_tensor(p1, p1, sc, ALU.subtract)
                                e.tensor_tensor(sc, qv1, snv, ALU.mult)
                                e.tensor_tensor(p2, p2, sc, ALU.add)
                                e.tensor_copy(q_sb[:, :], prod_sb[:, :])
                                kv1 = mkap(kv_sb[:, :], [[2 * D, 128], [128, 16], [1, 32]], 0)
                                kv2 = mkap(kv_sb[:, :], [[2 * D, 128], [128, 16], [1, 32]], 32)
                                k1 = mkap(k2_sb[:, :], [[2 * D, 128], [64, 16], [1, 32]], 0)
                                k2v = mkap(k2_sb[:, :], [[2 * D, 128], [64, 16], [1, 32]], 32)
                                e.tensor_tensor(k1, kv1, csv, ALU.mult)
                                e.tensor_tensor(k2v, kv2, csv, ALU.mult)
                                e.tensor_tensor(sc, kv2, snv, ALU.mult)
                                e.tensor_tensor(k1, k1, sc, ALU.subtract)
                                e.tensor_tensor(sc, kv1, snv, ALU.mult)
                                e.tensor_tensor(k2v, k2v, sc, ALU.add)
                                e.tensor_copy(k2_sb[:, 1024:2048], k2_sb[:, 0:1024])
                                vv_view = mkap(kv_sb[:, :], [[2 * D, 128], [128, 16], [1, 64]], 64)
                                e.tensor_copy(v2_sb[:, 0:1024], vv_view)
                                e.tensor_copy(v2_sb[:, 1024:2048], v2_sb[:, 0:1024])
                                for o in range(16):
                                    e.tensor_tensor(prod_sb[:, :], q_sb[:, :],
                                                    k2_sb[:, o * 64:o * 64 + 1024], ALU.mult)
                                    r = e.tensor_reduce(
                                        mkap(att_sb[:, :], [[256, 128], [16, 16]], o),
                                        mkap(prod_sb[:, :], [[D, 128], [64, 16], [1, 64]], 0),
                                        AX.X, ALU.add)
                                return r
                            v_qk = emit("vector", headmix, "lf")
                            wait("vector", "lf", v_qk)
                            v_mx = emit("vector", lambda e: e.tensor_reduce(
                                mx_sb[:, :],
                                mkap(att_sb[:, :], [[256, 128], [16, 16], [1, 16]], 0),
                                AX.X, ALU.max), "lf")
                            wait("vector", "lf", v_mx)
                            def submx(e):
                                for h in range(16):
                                    r = e.tensor_scalar(att_sb[:, h * 16:(h + 1) * 16],
                                                        att_sb[:, h * 16:(h + 1) * 16],
                                                        mx_sb[:, h:h + 1], None, ALU.subtract)
                                return r
                            v_hm1 = emit("vector", submx, "rp")
                            if i == 0:
                                tap(q_sb[:, :], 1024)
                                tap(k2_sb[:, 0:1024], 1024)
                                tap(att_sb[:, :], 256)
                            wait("scalar", "rp", v_hm1)
                            s_e = emit("scalar", lambda e: e.activation(
                                att_sb[:, :], att_sb[:, :], AF.Exp, bias=0.0, scale=0.125), "ev")
                            wait("vector", "ev", s_e)
                            v_sm = emit("vector", lambda e: e.tensor_reduce(
                                sm_sb[:, :],
                                mkap(att_sb[:, :], [[256, 128], [16, 16], [1, 16]], 0),
                                AX.X, ALU.add), "lf")
                            wait("vector", "lf", v_sm)
                            def headmix2(e):
                                e.reciprocal(rc_sb[:, :], sm_sb[:, :])
                                for o in range(16):
                                    ev = mkap(att_sb[:, :], [[256, 128], [16, 16], [0, 64]], o)
                                    if o == 0:
                                        e.tensor_tensor(acc_sb[:, :], v2_sb[:, 0:1024], ev, ALU.mult)
                                    else:
                                        e.tensor_tensor(prod_sb[:, :], v2_sb[:, o * 64:o * 64 + 1024],
                                                        ev, ALU.mult)
                                        e.tensor_tensor(acc_sb[:, :], acc_sb[:, :], prod_sb[:, :], ALU.add)
                                for h in range(16):
                                    r = e.tensor_scalar(outb_sb[:, h * 64:(h + 1) * 64],
                                                        acc_sb[:, h * 64:(h + 1) * 64],
                                                        rc_sb[:, h:h + 1], None, ALU.mult)
                                return r
                            v_hm2 = emit("vector", headmix2, "rp")
                            if i == 0:
                                tap(acc_sb[:, :], 1024)
                            wait("tensor", "rp", v_hm2)
                            wait("tensor", "ev2", semv["ev2"])
                            def trob(e):
                                for d8 in range(8):
                                    r = e.transpose(ps_t[:, d8 * 128:(d8 + 1) * 128],
                                                    outb_sb[:, d8 * 128:(d8 + 1) * 128], eye_s[:, :])
                                return r
                            t_ob = emit("tensor", trob, "tr")
                            wait("vector", "tr", t_ob)
                            wait("vector", "prj", wob_last)
                            ev_ob = emit("vector", lambda e: e.tensor_copy(obt_sb[:, :], ps_t[:, :]), "ev2")
                            wait("tensor", "ev2", ev_ob)
                            wait("tensor", "vx", semv["vx"])
                            def wobmm(e):
                                for d8 in range(8):
                                    for ch in range(2):
                                        e.matmul(ps_m[:, ch * 512:(ch + 1) * 512],
                                                 obt_sb[:, d8 * 128:(d8 + 1) * 128],
                                                 wob_s[:, d8 * D + ch * 512:d8 * D + (ch + 1) * 512],
                                                 start=(d8 == 0), stop=False)
                                e.matmul(ps_m[:, 0:512], ones_s[:, :], brow_s[0:1, 5 * D:5 * D + 512],
                                         start=False, stop=True)
                                return e.matmul(ps_m[:, 512:1024], ones_s[:, :],
                                                brow_s[0:1, 5 * D + 512:6 * D],
                                                start=False, stop=True)
                            t_wb = emit("tensor", wobmm, "prj")
                            wob_last = t_wb
                            wait("vector", "prj", t_wb)
                            emit("vector", lambda e, i=i: e.tensor_tensor(
                                x_sb[:, i * D:(i + 1) * D], ps_m[:, 0:1024],
                                x_sb[:, i * D:(i + 1) * D], ALU.add), "vx")

                barrier()
                if phases >= 4:
                    ffn_pass("fa")
                barrier()
                tap(x_sb[:, 0:1024], 1024)
                if phases >= 5:
                    attnb_pass()
                barrier()
                if phases >= 6:
                    ffn_pass("fc")
                barrier()

                for i in range(NT):
                    dma("sync", mkap(y_d, [[D, 128], [1, D]], i * 128 * D),
                        x_sb[:, i * D:(i + 1) * D], "yo")
                wait("sync", "yo", semv["yo"])

        if phases < 3:
            for i in range(NT):
                dma("sync", mkap(y_d, [[D, 128], [1, D]], i * 128 * D),
                    mkap(x_in, [[D, 128], [1, D]], i * 128 * D), "yo")
            wait("sync", "yo", semv["yo"])

        P.run(block)
    return nc


def _bucket_fp32(d):
    d = np.asarray(d, dtype=np.int64)
    rp = np.maximum(d, 0)
    max_exact = 16
    is_small = rp < max_exact
    rp_f = np.maximum(rp.astype(np.float32), np.float32(1.0))
    ratio = (rp_f / np.float32(max_exact)).astype(np.float32)
    lg = np.log(ratio, dtype=np.float32)
    denom = np.float32(np.log(128.0 / 16.0))
    val = (lg / denom).astype(np.float32) * np.float32(16)
    large = max_exact + val.astype(np.int32)
    large = np.minimum(large, 31)
    return np.where(is_small, rp, large).astype(np.int64)


_NC_CACHE = [None]


def prepare_in_maps(L0_attn_a_x, L0_attn_a_Wq, L0_attn_a_Wk, L0_attn_a_Wv, L0_attn_a_Wo,
           L0_attn_a_rel_emb, L0_attn_b_Wq, L0_attn_b_bq, L0_attn_b_Wdown,
           L0_attn_b_bdown, L0_attn_b_Wup, L0_attn_b_bup, L0_attn_b_Wout,
           L0_attn_b_bout, L0_ffn_W1, L0_ffn_b1, L0_ffn_W2, L0_ffn_b2):
    f32 = np.float32
    X = np.ascontiguousarray(np.asarray(L0_attn_a_x, f32).reshape(B * S, D))
    wq = np.ascontiguousarray(np.asarray(L0_attn_a_Wq, f32))
    wk = np.ascontiguousarray(np.asarray(L0_attn_a_Wk, f32))
    wv = np.ascontiguousarray(np.asarray(L0_attn_a_Wv, f32))
    wo = np.ascontiguousarray(np.asarray(L0_attn_a_Wo, f32))
    rel = np.ascontiguousarray(np.asarray(L0_attn_a_rel_emb, f32))
    w1t = np.ascontiguousarray(np.asarray(L0_ffn_W1, f32).T)
    w2t = np.ascontiguousarray(np.asarray(L0_ffn_W2, f32).T)
    b1 = np.asarray(L0_ffn_b1, f32)
    b2 = np.asarray(L0_ffn_b2, f32)
    wqb = np.ascontiguousarray(np.asarray(L0_attn_b_Wq, f32))
    wdn = np.ascontiguousarray(np.asarray(L0_attn_b_Wdown, f32))
    wup = np.ascontiguousarray(np.asarray(L0_attn_b_Wup, f32))
    wob = np.ascontiguousarray(np.asarray(L0_attn_b_Wout, f32))
    bq = np.asarray(L0_attn_b_bq, f32)
    bdn = np.asarray(L0_attn_b_bdown, f32)
    bup = np.asarray(L0_attn_b_bup, f32)
    bob = np.asarray(L0_attn_b_bout, f32)

    b1col = np.ascontiguousarray(b1.reshape(32, 128).T)
    brows = np.zeros((1, 6 * D), f32)
    brows[0, 0:D] = b2
    brows[0, D:2 * D] = bq
    brows[0, 2 * D:2 * D + DL] = bdn
    brows[0, 3 * D:5 * D] = bup
    brows[0, 5 * D:6 * D] = bob
    brows = brows.astype(BF)

    ii = np.arange(8448)
    bk = _bucket_fp32(np.maximum(ii - 127, 0))
    oh = np.zeros((32, 8448), f32)
    oh[bk, ii] = 1.0

    eye_b = np.eye(128, dtype=f32).astype(BF)
    revi_b = np.eye(128, dtype=f32)[::-1].copy().astype(BF)
    lt_b = np.triu(np.ones((128, 128), f32)).astype(BF)
    ones_b = np.ones((1, 128), f32).astype(BF)

    invf = (1.0 / (10000.0 ** (np.arange(0, HD, 2, dtype=f32) / HD))).astype(f32)
    tpos = np.arange(S, dtype=f32)
    th = np.outer(tpos, invf)
    cosT = np.cos(th).astype(f32)
    sinT = np.sin(th).astype(f32)

    in_maps = []
    for c in range(NCORE):
        rows = slice(2048 * c, 2048 * (c + 1))
        pos0 = 2048 * (c % 4)
        ct = cosT[pos0:pos0 + 2048].reshape(NT, 128, 32)
        stt = sinT[pos0:pos0 + 2048].reshape(NT, 128, 32)
        cos16 = np.repeat(ct[:, :, None, :], 16, axis=2).transpose(1, 0, 2, 3).reshape(128, NT * 512)
        sin16 = np.repeat(stt[:, :, None, :], 16, axis=2).transpose(1, 0, 2, 3).reshape(128, NT * 512)
        in_maps.append({
            "x_rows": np.ascontiguousarray(X[rows]),
            "wq": wq, "wk": wk, "wv": wv, "wo": wo,
            "w1t": w1t, "w2t": w2t,
            "b1col": b1col, "brows": brows,
            "wqb": wqb, "wdn": wdn, "wup": wup, "wob": wob,
            "rel2": np.ascontiguousarray(rel[:, 2 * c:2 * c + 2]),
            "oh": oh, "eye_b": eye_b, "revi_b": revi_b, "lt_b": lt_b,
            "ones_b": ones_b,
            "cos16": np.ascontiguousarray(cos16).astype(BF),
            "sin16": np.ascontiguousarray(sin16).astype(BF),
        })

    return in_maps


def kernel(**inputs):
    in_maps = prepare_in_maps(**inputs)
    if _NC_CACHE[0] is None:
        _NC_CACHE[0] = build()
    nc = _NC_CACHE[0]
    res = run_bass_kernel_spmd(nc, in_maps, core_ids=list(range(NCORE)))
    out = np.concatenate([np.asarray(res.results[c]["y"]) for c in range(NCORE)], axis=0)
    return out.reshape(B, S, D)
